# version 22
# baseline (speedup 1.0000x reference)
import contextlib

import numpy as np

import concourse.bass as bass
import concourse.mybir as mybir
from concourse.bass_utils import run_bass_kernel_spmd

F32 = mybir.dt.float32
BF16 = mybir.dt.bfloat16
I32 = mybir.dt.int32
AF = mybir.ActivationFunctionType
ALU = mybir.AluOpType
AX = mybir.AxisListType

D = 1024
S = 4096
STN = 2048
DFF = 4096
NE = 4
T = 512
MT = 4
KC = 8
JCH = 16
JF = 32
NCH_FULL = S // T

LN_EPS = 1e-5
RMS_EPS = 1e-8
HALF_C = float(-0.5 * np.log(512.0))

DMA_SEM_STEP = 16


class Prog:

    SERIAL = ("vector", "scalar")
    OWN = {"vector": "V", "scalar": "Sc"}

    def __init__(self):
        self.ops = []
        self.count = {}
        self.producer = {}
        self.step = {}

    def sem(self, name, producer, step):
        self.count[name] = 0
        self.producer[name] = producer
        self.step[name] = step

    def op(self, engine, fn, waits=(), inc=None):
        w = []
        for t in waits:
            if t is None:
                continue
            s, v = t
            if engine == "tensor" and self.producer[s] == "tensor":
                continue
            w.append((s, v))
        if engine in self.SERIAL:
            own = self.OWN[engine]
            if inc is None:
                inc = own
            prev = self.count[own]
            if prev > 0:
                w.append((own, prev))
        tok = None
        if inc is not None:
            assert (self.producer[inc] == engine
                    or ("tensor" not in (self.producer[inc], engine))), (inc, engine)
            self.count[inc] += self.step[inc]
            tok = (inc, self.count[inc])
        self.ops.append((engine, fn, w, inc))
        return tok

    def emit(self, block, sems):
        for en in ("sync", "gpsimd", "vector", "scalar", "tensor"):
            ops = [o for o in self.ops if o[0] == en]
            if not ops:
                continue

            def body(eng, ops=ops):
                waited = {}
                for (_, fn, waits, inc) in ops:
                    for (s, v) in waits:
                        if v > waited.get(s, 0):
                            eng.wait_ge(sems[s], v)
                            waited[s] = v
                    instr = fn(eng)
                    if inc is not None:
                        instr.then_inc(sems[inc], self.step[inc])

            getattr(block, en)(body)


def build_program(nch=NCH_FULL):
    nc = bass.Bass("TRN2", target_bir_lowering=False)
    seq = nch * T

    x_in = nc.dram_tensor("x_in", [seq, D], F32, kind="ExternalInput").ap()
    ls_in = nc.dram_tensor("ls_scale", [512], F32, kind="ExternalInput").ap()
    rs_in = nc.dram_tensor("rs_scale", [512], F32, kind="ExternalInput").ap()
    sp_in = nc.dram_tensor("split_scale", [D], F32, kind="ExternalInput").ap()
    wk_in = nc.dram_tensor("Wk", [D, STN], F32, kind="ExternalInput").ap()
    wugg_in = nc.dram_tensor("Wugg", [D, 3 * STN], F32, kind="ExternalInput").ap()
    wout_in = nc.dram_tensor("Wout", [STN, D], F32, kind="ExternalInput").ap()
    wr_in = nc.dram_tensor("Wr", [D, NE], F32, kind="ExternalInput").ap()
    br_in = nc.dram_tensor("br", [NE], F32, kind="ExternalInput").ap()
    w1_in = nc.dram_tensor("W1", [NE, D, DFF], F32, kind="ExternalInput").ap()
    b1_in = nc.dram_tensor("b1", [NE, DFF], F32, kind="ExternalInput").ap()
    w2_in = nc.dram_tensor("W2", [NE, DFF, D], F32, kind="ExternalInput").ap()
    b2_in = nc.dram_tensor("b2", [NE, D], F32, kind="ExternalInput").ap()
    y_out = nc.dram_tensor("y_out", [seq, D], F32, kind="ExternalOutput").ap()

    sWA = nc.dram_tensor("sWA", [JCH, 128, KC, 4, 128], BF16, kind="Internal").ap()
    sWout = nc.dram_tensor("sWout", [JCH, 128, D], BF16, kind="Internal").ap()
    sW1 = nc.dram_tensor("sW1", [NE, 8, 128, KC, 4, 128], BF16, kind="Internal").ap()
    sW2 = nc.dram_tensor("sW2", [NE, JF, 128, D], BF16, kind="Internal").ap()

    x_t = x_in.rearrange("(c t p) d -> c t p d", t=MT, p=128)
    y_t = y_out.rearrange("(c t p) d -> c t p d", t=MT, p=128)
    wk_t = wk_in.rearrange("(k p) c -> k p c", p=128)
    wugg_t = wugg_in.rearrange("(k p) c -> k p c", p=128)
    wout_t = wout_in.rearrange("(k p) c -> k p c", p=128)
    w1_t = w1_in.rearrange("e (k p) c -> e k p c", p=128)
    w1pt = w1_in.rearrange("e (k p) c -> e p k c", p=128)
    w2_t = w2_in.rearrange("e (k p) c -> e k p c", p=128)

    p = Prog()
    p.sem("dC", "sync", 16)
    for _s in range(4):
        p.sem(f"dP{_s}", "sync" if _s % 2 == 0 else "gpsimd", 16)
    for _s in range(4):
        p.sem(f"dT{_s}", "scalar", 16)
    for _s in range(4):
        p.sem(f"dZ{_s}", "gpsimd", 16)
    for _s in range(4):
        p.sem(f"dQ{_s}", "sync" if _s % 2 == 0 else "gpsimd", 16)
    for _s in range(4):
        p.sem(f"dU{_s}", "sync" if _s % 2 == 0 else "gpsimd", 16)
    p.sem("dX0", "gpsimd", 16)
    p.sem("dX1", "gpsimd", 16)
    p.sem("dY", "gpsimd", 16)
    p.sem("T", "tensor", 1)
    p.sem("V", "vector", 1)
    p.sem("Sc", "scalar", 1)
    p.sem("G", "gpsimd", 1)

    with contextlib.ExitStack() as stack:
        E = stack.enter_context

        x_sb = E(nc.sbuf_tensor("x_sb", [128, 2, MT, D], F32))
        acc = E(nc.sbuf_tensor("acc", [128, MT, D], F32))
        xn1t = E(nc.sbuf_tensor("xn1t", [128, 2, D], BF16))
        xnf = E(nc.sbuf_tensor("xnf", [128, KC, T], BF16))
        sqd16 = E(nc.sbuf_tensor("sqd16", [128, D], BF16))
        x2hi = E(nc.sbuf_tensor("x2hi", [128, 2, D], BF16))
        x2lo = E(nc.sbuf_tensor("x2lo", [128, 2, D], BF16))
        xnf2h = E(nc.sbuf_tensor("xnf2h", [128, KC, T], BF16))
        xnf2l = E(nc.sbuf_tensor("xnf2l", [128, KC, T], BF16))
        ktile = E(nc.sbuf_tensor("ktile", [128, 2, T + 1], F32))
        tmpgin = E(nc.sbuf_tensor("tmpgin", [128, 2, T], F32))
        u1t = E(nc.sbuf_tensor("u1t", [128, 2, T], F32))
        hneg = E(nc.sbuf_tensor("hneg", [128, 2, T], F32))
        gob = E(nc.sbuf_tensor("gob", [128, 2, T], BF16))
        outs_sb = E(nc.sbuf_tensor("outs_sb", [128, JCH, T], BF16))
        h1_sb = E(nc.sbuf_tensor("h1_sb", [128, JF, T], BF16))
        slab = E(nc.sbuf_tensor("slab", [128, 4, 4096], BF16))
        tmpev = E(nc.sbuf_tensor("tmpev", [128, 2, D], F32))
        stv = E(nc.sbuf_tensor("stv", [128, 96], F32))
        wsb = E(nc.sbuf_tensor("wsb", [128, 16], F32))
        rt4 = E(nc.sbuf_tensor("rt4", [128, 12], F32))
        wT_sb = E(nc.sbuf_tensor("wT_sb", [4, MT, 128], BF16))
        b2f = E(nc.sbuf_tensor("b2f", [4, D], F32))
        b2mv = E(nc.sbuf_tensor("b2mv", [4, D], BF16))
        b1s = E(nc.sbuf_tensor("b1s", [128, 128], F32))
        br_sb = E(nc.sbuf_tensor("br_sb", [128, NE], F32))
        wrh = E(nc.sbuf_tensor("wrh", [128, 32], BF16))
        wrl = E(nc.sbuf_tensor("wrl", [128, 32], BF16))
        wrf = E(nc.sbuf_tensor("wrf", [128, 32], F32))
        sc_sb = E(nc.sbuf_tensor("sc_sb", [128, KC], F32))
        lsrs = E(nc.sbuf_tensor("lsrs", [128, KC], F32))
        spv = E(nc.sbuf_tensor("spv", [128, KC], F32))
        idb = E(nc.sbuf_tensor("idb", [128, 128], BF16))
        idf = E(nc.sbuf_tensor("idf", [128, 128], F32))
        cst = E(nc.sbuf_tensor("cst", [128, 4], F32))
        kcar = E(nc.sbuf_tensor("kcar", [128, JCH], F32))
        hcar = E(nc.sbuf_tensor("hcar", [128, JCH], F32))
        b1v = acc[:, 0, 0:128]
        iii = acc[:, 1, 0:128].bitcast(I32)


        pb = [E(nc.psum_tensor(f"pb{i}", [128, 512], F32)) for i in range(8)]
        pbb = [pb[i][:].bitcast(BF16) for i in range(8)]

        sems = {n: E(nc.semaphore(n)) for n in
                ["dC", "dP0", "dP1", "dP2", "dP3",
                 "dT0", "dT1", "dT2", "dT3", "dZ0", "dZ1", "dZ2", "dZ3",
                 "dQ0", "dQ1", "dQ2", "dQ3", "dU0", "dU1", "dU2", "dU3",
                 "dX0", "dX1", "dY", "T", "V", "Sc", "G"]}

        h1b16 = h1_sb[:].rearrange("p a b -> p (a b)")

        bank_free = {i: None for i in range(8)}

        tG = p.op("gpsimd", lambda e: e.iota(iii, pattern=[[-1, 128]], base=0,
                                             channel_multiplier=1), inc="G")
        tIdb = p.op("vector", lambda e: e.tensor_single_scalar(idb[:], iii, 0.0, ALU.is_equal),
                    waits=[tG], inc="V")
        tIdf = p.op("vector", lambda e: e.tensor_single_scalar(idf[:], iii, 0.0, ALU.is_equal),
                    waits=[tG], inc="V")

        p.op("sync", lambda e: e.dma_start(allow_slow_non_contiguous=True, out=lsrs[:, 0:4],
                                           in_=ls_in.rearrange("(a p) -> p a", p=128)), inc="dC")
        p.op("sync", lambda e: e.dma_start(allow_slow_non_contiguous=True, out=lsrs[:, 4:8],
                                           in_=rs_in.rearrange("(a p) -> p a", p=128)), inc="dC")
        p.op("sync", lambda e: e.dma_start(allow_slow_non_contiguous=True, out=spv[:],
                                           in_=sp_in.rearrange("(a p) -> p a", p=128)), inc="dC")
        p.op("sync", lambda e: e.dma_start(out=br_sb[:],
                                           in_=br_in[None, :].broadcast_to([128, NE])), inc="dC")
        p.op("sync", lambda e: e.dma_start(out=b2f[:], in_=b2_in), inc="dC")
        p.op("sync", lambda e: e.dma_start(allow_slow_non_contiguous=True, out=wrf[:],
                                           in_=wr_in.rearrange("(k p) c -> p k c", p=128)), inc="dC")
        p.op("sync", lambda e: e.dma_start(out=b1v,
                                           in_=b1_in.rearrange("e (x y) -> (e x) y", y=128)), inc="dC")
        tCall = ("dC", p.count["dC"])

        tSc = p.op("vector", lambda e: e.tensor_mul(sc_sb[:], lsrs[:], spv[:]),
                   waits=[tCall], inc="V")
        tBr = tCall
        tB2c = p.op("vector", lambda e: e.tensor_copy(b2mv[:], b2f[:]), waits=[tCall], inc="V")
        tWrh = p.op("vector", lambda e: e.tensor_copy(wrh[:], wrf[:]), waits=[tCall], inc="V")
        tWrl = p.op("vector", lambda e: e.tensor_sub(wrl[:], wrf[:], wrh[:]), inc="V")
        tB1t = p.op("tensor", lambda e: e.transpose(pb[6][:, 0:128], b1v, idf[:]),
                    waits=[tCall, tIdf], inc="T")
        tB1s = p.op("scalar", lambda e: e.copy(b1s[:], pb[6][:, 0:128]), waits=[tB1t], inc="Sc")
        bank_free[6] = tB1s

        p.op("vector", lambda e: e.memset(cst[:, 0:1], LN_EPS), inc="V")
        p.op("vector", lambda e: e.memset(cst[:, 1:2], -0.5), inc="V")
        p.op("vector", lambda e: e.memset(cst[:, 2:3], 0.5), inc="V")
        p.op("vector", lambda e: e.memset(cst[:, 3:4], HALF_C), inc="V")
        p.op("vector", lambda e: e.memset(kcar[:], 1.0), inc="V")
        tHc = p.op("vector", lambda e: e.memset(hcar[:], 0.0), inc="V")

        region_done = {}

        def ensure_region(region):
            return [t for t in region_done[region] if t is not None]

        slot_free = [None, None, None, None]
        slot_rr = [0]

        def load_piece(in_ap, region):
            rt = ensure_region(region)
            s = slot_rr[0]
            slot_rr[0] = (s + 1) % 4
            eng = "sync" if s % 2 == 0 else "gpsimd"
            tok = p.op(eng, lambda e, s=s, in_ap=in_ap: e.dma_start(out=slab[:, s, :], in_=in_ap),
                       waits=[slot_free[s], store_free[s]] + list(rt), inc=f"dP{s}")
            return s, tok

        store_free = [None, None, None, None]

        def load_piece_cast(f32_ap, store_ap, region):
            s = slot_rr[0]
            slot_rr[0] = (s + 1) % 4
            tokL = p.op("gpsimd",
                        lambda e, s=s, f32_ap=f32_ap: e.dma_start(out=slab[:, s, :], in_=f32_ap),
                        waits=[slot_free[s], store_free[s]], inc=f"dZ{s}")
            tokS = p.op("scalar",
                        lambda e, s=s, store_ap=store_ap: e.dma_start(out=store_ap, in_=slab[:, s, :]),
                        waits=[tokL], inc=f"dT{s}")
            store_free[s] = tokS
            region_done.setdefault(region, [None] * 4)[s] = tokS
            return s, tokL

        wk_p = wk_in.rearrange("(k p) c -> p k c", p=128)
        wugg_p = wugg_in.rearrange("(k p) (w jc) -> p k w jc", p=128, w=3)

        def load_piece_cast_wa(j):
            s = slot_rr[0]
            slot_rr[0] = (s + 1) % 4
            slab4 = slab[:, s, :].rearrange("p (k w c) -> p k w c", w=4, c=128)
            p.op("gpsimd",
                 lambda e, slab4=slab4, j=j: e.dma_start(
                     out=slab4[:, :, 0, :], in_=wk_p[:, :, j * 128:(j + 1) * 128]),
                 waits=[slot_free[s], store_free[s]], inc=f"dZ{s}")
            tokL = None
            for w in range(3):
                tokL = p.op("gpsimd",
                            lambda e, slab4=slab4, j=j, w=w: e.dma_start(
                                out=slab4[:, :, w + 1, :],
                                in_=wugg_p[:, :, w, j * 128:(j + 1) * 128]),
                            inc=f"dZ{s}")
            tokS = p.op("scalar",
                        lambda e, s=s, j=j: e.dma_start(
                            out=sWA[j].rearrange("p k w c -> p (k w c)"),
                            in_=slab[:, s, :]),
                        waits=[tokL], inc=f"dT{s}")
            store_free[s] = tokS
            region_done.setdefault("WA", [None] * 4)[s] = tokS
            return s, tokL

        def free_slot(s, tok):
            assert tok is not None
            slot_free[s] = tok

        x_ready = [None] * nch
        chunk_done = [None] * nch
        acc_free = None
        attn_prev_v = [None, None]
        tr1_prev = [None, None]
        tr2_prev = [None, None]
        sc_stat_prev = [None, None]
        ln_v_prev = [None, None]
        tex_prev = [None]
        scale_prev = [None] * MT
        tw_m = [None] * MT

        SW = {}
        for i, name in enumerate(["Sl", "Sr", "Ql", "Qr", "Ss", "m", "E2", "msq",
                                  "var", "r", "nl", "nr", "s2l", "s2r", "t1", "t2",
                                  "mx", "mxn", "s2", "rs",
                                  "g01", "g02", "g03", "g12", "g13", "g23",
                                  "rk0", "rk1", "rk2", "rk3", "gl", "gr", "lt",
                                  "m0", "m1", "m2", "m3", "r0", "r1", "r2", "r3",
                                  "aa", "bb"]):
            SW[name] = i

        def sv(name, par):
            c = 2 * SW[name] + par
            return stv[:, c:c + 1]

        pre_ln = {}
        ln1_tap = [None] * MT
        h1_free = [None]

        def emit_ln1_tile(ch, mt):
            cb = ch % 2
            par = mt % 2
            xtile = x_sb[:, cb, mt, :]
            xl = xtile[:, 0:512]
            xr = xtile[:, 512:1024]
            p.op("scalar", lambda e, xl=xl, par=par:
                 e.activation(sqd16[:, 0:512], xl, AF.Square, accum_out=sv("Ql", par)),
                 waits=[x_ready[ch], ln_v_prev[par]])
            tsq = p.op("scalar", lambda e, xr=xr, par=par:
                       e.activation(sqd16[:, 512:1024], xr, AF.Square, accum_out=sv("Qr", par)),
                       inc="Sc")
            p.op("vector", lambda e, xl=xl, par=par: e.reduce_sum(sv("Sl", par), xl, axis=AX.X),
                 waits=[x_ready[ch], sc_stat_prev[par]])
            p.op("vector", lambda e, xr=xr, par=par: e.reduce_sum(sv("Sr", par), xr, axis=AX.X))
            p.op("vector", lambda e, par=par: e.tensor_add(sv("Ss", par), sv("Sl", par), sv("Sr", par)))
            p.op("vector", lambda e, par=par, mt=mt: e.tensor_single_scalar(sv("m" + str(mt), 0), sv("Ss", par), 1.0 / D, ALU.mult))
            p.op("vector", lambda e, par=par: e.tensor_add(sv("E2", par), sv("Ql", par), sv("Qr", par)),
                 waits=[tsq])
            p.op("vector", lambda e, par=par: e.tensor_single_scalar(sv("E2", par), sv("E2", par), 1.0 / D, ALU.mult))
            p.op("vector", lambda e, par=par, mt=mt: e.tensor_mul(sv("msq", par), sv("m" + str(mt), 0), sv("m" + str(mt), 0)))
            tv = p.op("vector", lambda e, par=par: e.tensor_sub(sv("var", par), sv("E2", par), sv("msq", par)), inc="V")
            ln_v_prev[par] = tv
            p.op("scalar", lambda e, par=par: e.activation(sv("lt", par), sv("var", par), AF.Ln, bias=cst[:, 0:1]),
                 waits=[tv])
            tr_ = p.op("scalar", lambda e, par=par, mt=mt: e.activation(sv("r" + str(mt), 0), sv("lt", par), AF.Exp, scale=cst[:, 1:2]),
                       inc="Sc")
            tq = None
            for (qn, sn, s2n) in (("Ql", "Sl", "s2l"), ("Qr", "Sr", "s2r")):
                p.op("vector", lambda e, par=par, sn=sn, mt=mt:
                     e.scalar_tensor_tensor(sv("t1", par), sv("m" + str(mt), 0), -2.0, sv(sn, par), ALU.mult, ALU.mult))
                p.op("vector", lambda e, par=par, mt=mt:
                     e.scalar_tensor_tensor(sv("t2", par), sv("m" + str(mt), 0), 512.0, sv("m" + str(mt), 0), ALU.mult, ALU.mult))
                p.op("vector", lambda e, par=par, qn=qn, s2n=s2n:
                     e.tensor_add(sv(s2n, par), sv("t1", par), sv(qn, par)))
                tq = p.op("vector", lambda e, par=par, s2n=s2n:
                          e.tensor_add(sv(s2n, par), sv(s2n, par), sv("t2", par)), inc="V")
            p.op("scalar", lambda e, par=par: e.activation(sv("nl", par), sv("s2l", par), AF.Ln), waits=[tq])
            p.op("scalar", lambda e, par=par: e.activation(sv("nl", par), sv("nl", par), AF.Exp, scale=cst[:, 2:3], bias=cst[:, 3:4]))
            p.op("scalar", lambda e, par=par: e.activation(sv("nr", par), sv("s2r", par), AF.Ln))
            ts2 = p.op("scalar", lambda e, par=par: e.activation(sv("nr", par), sv("nr", par), AF.Exp, scale=cst[:, 2:3], bias=cst[:, 3:4]),
                       inc="Sc")
            sc_stat_prev[par] = ts2
            for (nn, gn) in (("nl", "gl"), ("nr", "gr")):
                p.op("vector", lambda e, par=par, nn=nn, mt=mt: e.tensor_mul(sv(nn, par), sv(nn, par), sv("r" + str(mt), 0)),
                     waits=[tr_, ts2])
                p.op("vector", lambda e, par=par, nn=nn: e.tensor_single_scalar(sv(nn, par), sv(nn, par), RMS_EPS, ALU.add))
                p.op("vector", lambda e, par=par, nn=nn: e.reciprocal(sv(nn, par), sv(nn, par)))
                p.op("vector", lambda e, par=par, nn=nn, gn=gn, mt=mt: e.tensor_mul(sv(gn, par), sv(nn, par), sv("r" + str(mt), 0)))
            p.op("vector", lambda e, xl=xl, par=par, mt=mt:
                 e.tensor_scalar(xn1t[:, par, 0:512], xl, sv("m" + str(mt), 0), sv("gl", par), ALU.subtract, ALU.mult),
                 waits=[tr1_prev[par]])
            tap = p.op("vector", lambda e, xr=xr, par=par, mt=mt:
                       e.tensor_scalar(xn1t[:, par, 512:1024], xr, sv("m" + str(mt), 0), sv("gr", par),
                                       ALU.subtract, ALU.mult), inc="V")
            p.op("vector", lambda e, par=par, mt=mt:
                 e.tensor_single_scalar(sv("aa", par), sv("r" + str(mt), 0), 1.0, ALU.add))
            p.op("vector", lambda e, par=par, mt=mt:
                 e.tensor_mul(sv("bb", par), sv("m" + str(mt), 0), sv("r" + str(mt), 0)))
            p.op("vector", lambda e, xtile=xtile, par=par:
                 e.tensor_scalar(xtile, xtile, sv("aa", par), sv("bb", par), ALU.mult, ALU.subtract))
            return tap

        def emit_ln1_tr(mt, tap):
            par = mt % 2
            ev = None
            for half in range(2):
                bk = 6 + half
                tk = None
                for q in range(4):
                    kc = half * 4 + q
                    tk = p.op("tensor",
                              lambda e, par=par, kc=kc, q=q, bk=bk:
                              e.transpose(pbb[bk][:, q * 128:(q + 1) * 128],
                                          xn1t[:, par, kc * 128:(kc + 1) * 128], idb[:]),
                              waits=[tap, tIdb, bank_free[bk]] if q == 0 else [],
                              inc="T" if q == 3 else None)
                for q in range(4):
                    kc = half * 4 + q
                    ev = p.op("scalar",
                              lambda e, kc=kc, q=q, mt=mt, bk=bk:
                              e.activation(xnf[:, kc, mt * 128:(mt + 1) * 128],
                                           pbb[bk][:, q * 128:(q + 1) * 128],
                                           AF.Copy, scale=sc_sb[:, kc:kc + 1]),
                              waits=[tk, tSc] if q == 0 else [], inc="Sc")
                bank_free[bk] = ev
            tr1_prev[par] = tk
            return ev

        def load_x(ch, waits):
            buf = ch % 2
            toks = []
            for mt in range(MT):
                toks.append(p.op("gpsimd",
                                 lambda e, ch=ch, mt=mt, buf=buf:
                                 e.dma_start(out=x_sb[:, buf, mt, :], in_=x_t[ch, mt]),
                                 waits=waits, inc=f"dX{buf}"))
            x_ready[ch] = toks[-1]

        load_x(0, [])
        if nch > 1:
            load_x(1, [])

        last_store = None

        for ch in range(nch):
            cb = ch % 2

            def xs(mt, cb=cb):
                return x_sb[:, cb, mt, :]

            if ch in pre_ln:
                xnf_ready = pre_ln.pop(ch)
            else:
                xnf_ready = None
                for mt in range(MT):
                    xnf_ready = emit_ln1_tr(mt, emit_ln1_tile(ch, mt))

            wa_h1_tok = [None] * 4
            if ch > 0:
                rtWA = ensure_region("WA")
                for j4 in range(4):
                    wa_h1_tok[j4] = p.op(
                        "sync" if j4 % 2 == 0 else "gpsimd",
                        lambda e, j4=j4: e.dma_start(
                            out=h1b16[:, j4 * 4096:(j4 + 1) * 4096],
                            in_=sWA[j4].rearrange("p k w c -> p (k w c)")),
                        waits=[h1_free[0]] + list(rtWA), inc=f"dQ{j4}")

            attn_ev_s = [None] * JCH
            attn_ev_v = [None] * JCH
            tmm_j3 = None
            wo_slots = []
            use_h1_wo = (ch > 0)
            for j in range(JCH):
                jb = j % 2
                pbase = 4 - jb * 4
                if ch > 0 and j < 4:
                    sl, tokL = None, wa_h1_tok[j]
                elif ch == 0:
                    sl, tokL = load_piece_cast_wa(j)
                else:
                    sl, tokL = load_piece(sWA[j].rearrange("p k w c -> p (k w c)"), "WA")
                if use_h1_wo and j == 10:
                    rtWO = ensure_region("WO")
                    for g in range(4):
                        tokWO = p.op(
                            "sync" if g % 2 == 0 else "gpsimd",
                            lambda e, g=g: e.dma_start(
                                out=h1b16[:, g * 4096:(g + 1) * 4096],
                                in_=sWout[4 * g:4 * g + 4].transpose([1, 0, 2])),
                            waits=list(rtWO) + [tmm_j3], inc=f"dU{g}")
                        wo_slots.append((None, tokWO))
                waits = [xnf_ready, tokL]
                if j >= 2:
                    waits += [attn_ev_s[j - 2], attn_ev_v[j - 2]]
                else:
                    waits += [bank_free[pbase + i] for i in range(4)]
                tmm = None
                for w in range(4):
                    for kc in range(KC):
                        if sl is None:
                            lhs_ap = h1b16[:, j * 4096 + (kc * 4 + w) * 128:
                                           j * 4096 + (kc * 4 + w) * 128 + 128]
                        else:
                            lhs_ap = slab[:, sl, (kc * 4 + w) * 128:(kc * 4 + w) * 128 + 128]
                        tmm = p.op("tensor",
                                   lambda e, lhs_ap=lhs_ap, w=w, kc=kc, pbase=pbase:
                                   e.matmul(pb[pbase + w][:, 0:512],
                                            lhsT=lhs_ap,
                                            rhs=xnf[:, kc, :],
                                            start=(kc == 0), stop=(kc == KC - 1)),
                                   waits=waits if (w == 0 and kc == 0) else [],
                                   inc="T" if (w == 3 and kc == KC - 1) else None)
                if j == 3:
                    tmm_j3 = tmm
                if sl is not None:
                    free_slot(sl, tmm)
                swait = [tmm]
                if j >= 2:
                    swait.append(attn_ev_v[j - 2])
                else:
                    swait.append(attn_prev_v[j])
                p.op("scalar", lambda e, jb=jb, pbase=pbase:
                     e.activation(ktile[:, jb, 1:T + 1], pb[pbase + 0][:, 0:512], AF.Sigmoid),
                     waits=swait)
                p.op("scalar", lambda e, jb=jb, pbase=pbase:
                     e.activation(tmpgin[:, jb, :], pb[pbase + 2][:, 0:512], AF.Sigmoid))
                tse = p.op("scalar", lambda e, jb=jb, pbase=pbase:
                           e.activation(gob[:, jb, :], pb[pbase + 3][:, 0:512], AF.Sigmoid), inc="Sc")
                attn_ev_s[j] = tse
                p.op("vector", lambda e, jb=jb, j=j: e.tensor_copy(ktile[:, jb, 0:1], kcar[:, j:j + 1]),
                     waits=[tse, tHc])
                p.op("vector", lambda e, jb=jb, pbase=pbase:
                     e.tensor_mul(u1t[:, jb, :], pb[pbase + 1][:, 0:512], tmpgin[:, jb, :]))
                p.op("vector", lambda e, jb=jb:
                     e.scalar_tensor_tensor(u1t[:, jb, :], ktile[:, jb, 1:T + 1], 1.0, u1t[:, jb, :],
                                            ALU.subtract, ALU.mult))
                p.op("vector", lambda e, jb=jb, j=j:
                     e.tensor_tensor_scan(hneg[:, jb, :], ktile[:, jb, 0:T], u1t[:, jb, :],
                                          hcar[:, j:j + 1], ALU.mult, ALU.add))
                p.op("vector", lambda e, jb=jb, j=j: e.tensor_copy(hcar[:, j:j + 1], hneg[:, jb, T - 1:T]))
                p.op("vector", lambda e, jb=jb, j=j: e.tensor_copy(kcar[:, j:j + 1], ktile[:, jb, T:T + 1]))
                attn_ev_v[j] = p.op("vector", lambda e, jb=jb, j=j:
                                    e.scalar_tensor_tensor(outs_sb[:, j, :], hneg[:, jb, :], -1.0, gob[:, jb, :],
                                                           ALU.mult, ALU.mult), inc="V")
            attn_prev_v = [attn_ev_v[JCH - 2], attn_ev_v[JCH - 1]]
            for b in range(8):
                bank_free[b] = attn_ev_v[JCH - 1] if (b % 4) == 1 else attn_ev_s[JCH - 1]
            bank_free[1] = attn_ev_v[JCH - 1]
            bank_free[5] = attn_ev_v[JCH - 1]

            if not use_h1_wo:
                for g in range(4):
                    sl, tokL = load_piece_cast(
                        wout_t[4 * g:4 * g + 4].transpose([1, 0, 2]),
                        sWout[4 * g:4 * g + 4].transpose([1, 0, 2]), "WO")
                    wo_slots.append((sl, tokL))
            xp_tok = [None] * MT
            xnf2_ready = [None] * MT
            tmm4 = None

            def emit_tr5(mt, thl):
                par = mt % 2
                ev = None
                tk = None
                for (src, dst) in ((x2hi, xnf2h), (x2lo, xnf2l)):
                    for half in range(2):
                        bk = 6 + half
                        for q in range(4):
                            kc = half * 4 + q
                            tk = p.op("tensor",
                                      lambda e, par=par, kc=kc, q=q, bk=bk, src=src:
                                      e.transpose(pbb[bk][:, q * 128:(q + 1) * 128],
                                                  src[:, par, kc * 128:(kc + 1) * 128], idb[:]),
                                      waits=[thl, bank_free[bk]] if q == 0 else [],
                                      inc="T" if q == 3 else None)
                        ev = p.op("scalar",
                                  lambda e, half=half, mt=mt, bk=bk, dst=dst:
                                  e.copy(dst[:, half * 4:half * 4 + 4, mt * 128:(mt + 1) * 128],
                                         pbb[bk][:, 0:512].rearrange("p (q c) -> p q c", q=4)),
                                  waits=[tk], inc="Sc")
                        bank_free[bk] = ev
                xnf2_ready[mt] = ev
                tr2_prev[par] = tk

            thl_m = [None] * MT
            for m in range(MT):
                par = m % 2
                pb4 = (m % 2) * 2
                for k in range(JCH):
                    sl, tokL = wo_slots[k // 4]
                    kk = k % 4
                    for h in range(2):
                        w = []
                        if k == 0 and h == 0:
                            if m == 0:
                                w = [attn_ev_v[JCH - 2], attn_ev_s[JCH - 2],
                                     wo_slots[0][1]]
                            w += [bank_free[pb4], bank_free[pb4 + 1]]
                        elif m == 0 and kk == 0 and h == 0:
                            w = [wo_slots[k // 4][1]]
                        if m == 0 and k == JCH - 1 and h == 0:
                            w = w + [attn_ev_v[JCH - 1]]
                        if use_h1_wo:
                            g_ = k // 4
                            rhs_ap = h1b16[:, g_ * 4096 + kk * 1024 + h * 512:
                                           g_ * 4096 + kk * 1024 + h * 512 + 512]
                        else:
                            rhs_ap = slab[:, sl, kk * 1024 + h * 512: kk * 1024 + h * 512 + 512]
                        tmm4 = p.op("tensor",
                                    lambda e, rhs_ap=rhs_ap, k=k, m=m, h=h:
                                    e.matmul(pb[(0 if m % 2 == 0 else 2) + h][:, 0:512],
                                             lhsT=outs_sb[:, k, m * 128:(m + 1) * 128],
                                             rhs=rhs_ap,
                                             start=(k == 0), stop=(k == JCH - 1)),
                                    waits=w,
                                    inc="T" if (k == JCH - 1 and h == 1) else None)
                p.op("vector", lambda e, m=m, cb=cb, pb4=pb4:
                     e.tensor_add(x_sb[:, cb, m, 0:512], x_sb[:, cb, m, 0:512], pb[pb4][:, 0:512]),
                     waits=[tmm4])
                xp_tok[m] = p.op("vector", lambda e, m=m, cb=cb, pb4=pb4:
                                 e.tensor_add(x_sb[:, cb, m, 512:1024], x_sb[:, cb, m, 512:1024],
                                              pb[pb4 + 1][:, 0:512]), inc="V")
                bank_free[pb4] = xp_tok[m]
                bank_free[pb4 + 1] = xp_tok[m]
                xt_ = xs(m)
                tsq5 = p.op("scalar", lambda e, xt_=xt_, par=par:
                            e.activation(sqd16[:], xt_, AF.Square, accum_out=sv("Ql", par)),
                            waits=[xp_tok[m], ln_v_prev[par]], inc="Sc")
                p.op("vector", lambda e, xt_=xt_, par=par: e.reduce_sum(sv("Ss", par), xt_, axis=AX.X),
                     waits=[xp_tok[m], sc_stat_prev[par]])
                p.op("vector", lambda e, par=par: e.tensor_single_scalar(sv("m", par), sv("Ss", par), 1.0 / D, ALU.mult))
                p.op("vector", lambda e, par=par: e.tensor_single_scalar(sv("E2", par), sv("Ql", par), 1.0 / D, ALU.mult),
                     waits=[tsq5])
                p.op("vector", lambda e, par=par: e.tensor_mul(sv("msq", par), sv("m", par), sv("m", par)))
                tv = p.op("vector", lambda e, par=par: e.tensor_sub(sv("var", par), sv("E2", par), sv("msq", par)), inc="V")
                ln_v_prev[par] = tv
                p.op("scalar", lambda e, par=par: e.activation(sv("lt", par), sv("var", par), AF.Ln, bias=cst[:, 0:1]),
                     waits=[tv])
                tr_ = p.op("scalar", lambda e, par=par: e.activation(sv("r", par), sv("lt", par), AF.Exp, scale=cst[:, 1:2]),
                           inc="Sc")
                sc_stat_prev[par] = tr_
                p.op("vector", lambda e, xt_=xt_, par=par:
                     e.tensor_scalar(tmpev[:, par, :], xt_, sv("m", par), sv("r", par), ALU.subtract, ALU.mult),
                     waits=[tr_, tr2_prev[par]])
                p.op("vector", lambda e, par=par: e.tensor_copy(x2hi[:, par, :], tmpev[:, par, :]))
                thl_m[m] = p.op("vector", lambda e, par=par: e.tensor_sub(x2lo[:, par, :], tmpev[:, par, :], x2hi[:, par, :]),
                                inc="V")
                if m >= 1:
                    emit_tr5(m - 1, thl_m[m - 1])
            for sl, _ in wo_slots:
                if sl is not None:
                    free_slot(sl, tmm4)
            p6a_pending = True

            racc_tok = [None] * MT

            def p6a(m):
                combos = [(xnf2h, wrh), (xnf2l, wrh), (xnf2h, wrl)]
                tmm6 = None
                for ci, (xa, wb) in enumerate(combos):
                    for kc in range(KC):
                        first = (ci == 0 and kc == 0)
                        last = (ci == 2 and kc == KC - 1)
                        w = [xnf2_ready[m], bank_free[0], tWrh, tWrl] if first else []
                        tmm6 = p.op("tensor",
                                    lambda e, xa=xa, wb=wb, kc=kc, m=m, first=first, last=last:
                                    e.matmul(pb[0][:, 0:4],
                                             lhsT=xa[:, kc, m * 128:(m + 1) * 128],
                                             rhs=wb[:, kc * 4:kc * 4 + 4],
                                             start=first, stop=last),
                                    waits=w,
                                    inc="T" if last else None)
                lg = rt4[:, 0:4]
                ex = rt4[:, 4:8]
                mk = rt4[:, 8:12]
                tlg = p.op("vector", lambda e, lg=lg: e.tensor_add(lg, pb[0][:, 0:4], br_sb[:]),
                           waits=[tmm6, tBr, tex_prev[0]], inc="V")
                bank_free[0] = tlg
                p.op("vector", lambda e, lg=lg: e.reduce_max(sv("mx", 0), lg, axis=AX.X))
                tmx = p.op("vector", lambda e: e.tensor_single_scalar(sv("mxn", 0), sv("mx", 0), -1.0, ALU.mult),
                           inc="V")
                tex = p.op("scalar", lambda e, lg=lg, ex=ex: e.activation(ex, lg, AF.Exp, bias=sv("mxn", 0)),
                           waits=[tmx], inc="Sc")
                tex_prev[0] = tex
                for (a, b, gn) in ((0, 1, "g01"), (0, 2, "g02"), (0, 3, "g03"),
                                   (1, 2, "g12"), (1, 3, "g13"), (2, 3, "g23")):
                    p.op("vector", lambda e, lg=lg, a=a, b=b, gn=gn:
                         e.tensor_tensor(sv(gn, 0), lg[:, a:a + 1], lg[:, b:b + 1], ALU.is_ge))
                p.op("vector", lambda e: e.tensor_add(sv("rk0", 0), sv("g01", 0), sv("g02", 0)))
                p.op("vector", lambda e: e.tensor_add(sv("rk0", 0), sv("rk0", 0), sv("g03", 0)))
                p.op("vector", lambda e: e.tensor_single_scalar(sv("rk0", 0), sv("rk0", 0), -1.0, ALU.mult))
                p.op("vector", lambda e: e.tensor_single_scalar(sv("rk0", 0), sv("rk0", 0), 3.0, ALU.add))
                p.op("vector", lambda e: e.tensor_sub(sv("rk1", 0), sv("g12", 0), sv("g01", 0)))
                p.op("vector", lambda e: e.tensor_add(sv("rk1", 0), sv("rk1", 0), sv("g13", 0)))
                p.op("vector", lambda e: e.tensor_single_scalar(sv("rk1", 0), sv("rk1", 0), -1.0, ALU.mult))
                p.op("vector", lambda e: e.tensor_single_scalar(sv("rk1", 0), sv("rk1", 0), 2.0, ALU.add))
                p.op("vector", lambda e: e.tensor_add(sv("rk2", 0), sv("g02", 0), sv("g12", 0)))
                p.op("vector", lambda e: e.tensor_sub(sv("rk2", 0), sv("rk2", 0), sv("g23", 0)))
                p.op("vector", lambda e: e.tensor_single_scalar(sv("rk2", 0), sv("rk2", 0), 1.0, ALU.add))
                p.op("vector", lambda e: e.tensor_add(sv("rk3", 0), sv("g03", 0), sv("g13", 0)))
                p.op("vector", lambda e: e.tensor_add(sv("rk3", 0), sv("rk3", 0), sv("g23", 0)))
                for i_, rk in enumerate(("rk0", "rk1", "rk2", "rk3")):
                    p.op("vector", lambda e, mk=mk, i_=i_, rk=rk:
                         e.tensor_single_scalar(mk[:, i_:i_ + 1], sv(rk, 0), 1.5, ALU.is_lt))
                p.op("vector", lambda e, mk=mk, ex=ex: e.tensor_mul(mk, mk, ex), waits=[tex])
                p.op("vector", lambda e, mk=mk: e.reduce_sum(sv("s2", 0), mk, axis=AX.X))
                p.op("vector", lambda e: e.reciprocal(sv("rs", 0), sv("s2", 0)))
                tw_m[m] = p.op("vector", lambda e, mk=mk, m=m:
                               e.tensor_scalar_mul(wsb[:, m * 4:m * 4 + 4], mk, sv("rs", 0)),
                               waits=[scale_prev[m]], inc="V")

            def p6b():
                for m in range(MT):
                    twt = p.op("tensor", lambda e, m=m: e.transpose(pb[1][0:4, 0:128], wsb[:, m * 4:m * 4 + 4], idf[:]),
                               waits=[tw_m[m], bank_free[1], tIdf], inc="T")
                    twte = p.op("vector", lambda e, m=m: e.tensor_copy(wT_sb[0:4, m, :], pb[1][0:4, 0:128]),
                                waits=[twt], inc="V")
                    bank_free[1] = twte
                    tb2 = [None, None]
                    for h in range(2):
                        tb2[h] = p.op("tensor", lambda e, m=m, h=h:
                                      e.matmul(pb[2 + h][:, 0:512], lhsT=wT_sb[0:4, m, :],
                                               rhs=b2mv[0:4, h * 512:h * 512 + 512], start=True, stop=True),
                                      waits=[twte, bank_free[2 + h], tB2c], inc="T")
                    aw = [tb2[0], tb2[1]]
                    if acc_free is not None:
                        aw.append(acc_free)
                    p.op("vector", lambda e, m=m, cb=cb:
                         e.tensor_add(acc[:, m, 0:512], x_sb[:, cb, m, 0:512], pb[2][:, 0:512]),
                         waits=aw)
                    racc_tok[m] = p.op("vector", lambda e, m=m, cb=cb:
                                       e.tensor_add(acc[:, m, 512:1024], x_sb[:, cb, m, 512:1024], pb[3][:, 0:512]),
                                       inc="V")
                    bank_free[2] = racc_tok[m]
                    bank_free[3] = racc_tok[m]
                chunk_done[ch] = racc_tok[MT - 1]
                if ch + 2 < nch:
                    load_x(ch + 2, [chunk_done[ch]])

            for m in range(MT - 1):
                p6a(m)
            emit_tr5(MT - 1, thl_m[MT - 1])
            p6a(MT - 1)

            gel_tok = [None] * JF
            prev_ev = [None] * MT
            for e_ in range(NE):
                for jg in range(8):
                    if ch == 0:
                        sl, tokL = load_piece_cast(
                            w1pt[e_][:, :, jg * 512:(jg + 1) * 512],
                            sW1[e_, jg].rearrange("p k jp c -> p (k jp c)"), f"W1{e_}")
                    else:
                        sl, tokL = load_piece(sW1[e_, jg].rearrange("p k jp c -> p (k jp c)"), f"W1{e_}")
                    tmm7 = None
                    for jp in range(4):
                        j = jg * 4 + jp
                        bk = 4 + j % 2
                        w = [tokL, xnf2_ready[MT - 1], bank_free[bk]]
                        if j >= 2:
                            w.append(gel_tok[j - 2])
                        for kc in range(KC):
                            tmm7 = p.op("tensor",
                                        lambda e, sl=sl, kc=kc, jp=jp, bk=bk:
                                        e.matmul(pb[bk][:, 0:512],
                                                 lhsT=slab[:, sl, (kc * 4 + jp) * 128:(kc * 4 + jp) * 128 + 128],
                                                 rhs=xnf2h[:, kc, :],
                                                 start=(kc == 0), stop=(kc == KC - 1)),
                                        waits=w if kc == 0 else [],
                                        inc="T" if kc == KC - 1 else None)
                        gw = [tmm7, tB1s]
                        gel_tok[j] = p.op("scalar",
                                          lambda e, j=j, e_=e_, bk=bk:
                                          e.activation(h1_sb[:, j, :], pb[bk][:, 0:512], AF.Gelu,
                                                       bias=b1s[:, e_ * 32 + j:e_ * 32 + j + 1]),
                                          waits=gw, inc="Sc")
                        bank_free[bk] = gel_tok[j]
                    free_slot(sl, tmm7)
                if e_ == 0:
                    p6b()
                if ch + 1 < nch:
                    sched = {0: (("t", 0), ("t", 1)),
                             1: (("r", 0), ("t", 2), ("r", 1)),
                             2: (("t", 3), ("r", 2)),
                             3: (("r", 3),)}
                    for kind, mt_ in sched.get(e_, ()):
                        if kind == "t":
                            ln1_tap[mt_] = emit_ln1_tile(ch + 1, mt_)
                        else:
                            pre_ln[ch + 1] = emit_ln1_tr(mt_, ln1_tap[mt_])
                tmm8 = None
                for g in range(8):
                    if ch == 0:
                        sl, tokL = load_piece_cast(
                            w2_t[e_, 4 * g:4 * g + 4].transpose([1, 0, 2]),
                            sW2[e_, 4 * g:4 * g + 4].transpose([1, 0, 2]), f"W2{e_}")
                    else:
                        sl, tokL = load_piece(sW2[e_, 4 * g:4 * g + 4].transpose([1, 0, 2]), f"W2{e_}")
                    for jj in range(4):
                        j = 4 * g + jj
                        for m in range(MT):
                            for h in range(2):
                                w = []
                                if jj == 0 and m == 0 and h == 0:
                                    w = [tokL, gel_tok[JF - 1] if g > 0 else gel_tok[j]]
                                    if g == 0:
                                        w += [bank_free[b] for b in range(8)]
                                        w += [t for t in prev_ev if t is not None]
                                elif m == 0 and h == 0:
                                    w = [gel_tok[j]] if g == 0 else []
                                tmm8 = p.op("tensor",
                                            lambda e, sl=sl, j=j, jj=jj, m=m, h=h:
                                            e.matmul(pb[m * 2 + h][:, 0:512],
                                                     lhsT=h1_sb[:, j, m * 128:(m + 1) * 128],
                                                     rhs=slab[:, sl, jj * 1024 + h * 512:jj * 1024 + h * 512 + 512],
                                                     start=(j == 0), stop=(j == JF - 1)),
                                            waits=w,
                                            inc="T" if (jj == 3 and m == MT - 1 and h == 1) else None)
                    free_slot(sl, tmm8)
                last_tad = {0: None, 1: None}
                for m in ((3, 2, 0, 1) if e_ == NE - 1 else (2, 3, 0, 1)):
                    par = m % 2
                    tsc_w = [tmm8, tw_m[m], last_tad[par]]
                    if prev_ev[m] is not None:
                        tsc_w.append(prev_ev[m])
                    p.op("scalar", lambda e, m=m, e_=e_, par=par:
                         e.activation(tmpev[:, par, 0:512], pb[m * 2][:, 0:512], AF.Copy,
                                      scale=wsb[:, m * 4 + e_:m * 4 + e_ + 1]),
                         waits=tsc_w)
                    tsc1 = p.op("scalar", lambda e, m=m, e_=e_, par=par:
                                e.activation(tmpev[:, par, 512:1024], pb[m * 2 + 1][:, 0:512], AF.Copy,
                                             scale=wsb[:, m * 4 + e_:m * 4 + e_ + 1]),
                                inc="Sc")
                    scale_prev[m] = tsc1
                    tad = p.op("vector", lambda e, m=m, par=par:
                               e.tensor_add(acc[:, m, :], acc[:, m, :], tmpev[:, par, :]),
                               waits=[tsc1, racc_tok[m]], inc="V")
                    prev_ev[m] = tad
                    last_tad[par] = tad
                    bank_free[m * 2] = tad
                    bank_free[m * 2 + 1] = tad

            h1_free[0] = tmm8

            st = None
            for mt in range(MT):
                st = p.op("gpsimd", lambda e, ch=ch, mt=mt:
                          e.dma_start(out=y_t[ch, mt], in_=acc[:, mt, :]),
                          waits=[prev_ev[mt]], inc="dY")
            acc_free = st
            last_store = st

        p.op("gpsimd", lambda e: e.nop(), waits=[last_store])

        with nc.Block() as block:
            p.emit(block, sems)

    return nc


_CACHE = {}


def _get_program(nch):
    if nch not in _CACHE:
        _CACHE[nch] = build_program(nch)
    return _CACHE[nch]


def kernel(L0_attn_x, L0_attn_ls_scale, L0_attn_rs_scale, L0_attn_split_scale,
           L0_attn_Wk, L0_attn_Wugg, L0_attn_Wout,
           L0_ffn_Wr, L0_ffn_br, L0_ffn_W1, L0_ffn_b1, L0_ffn_W2, L0_ffn_b2):
    x = np.ascontiguousarray(np.asarray(L0_attn_x, dtype=np.float32))
    B = x.shape[0]
    weights = {
        "ls_scale": L0_attn_ls_scale, "rs_scale": L0_attn_rs_scale,
        "split_scale": L0_attn_split_scale, "Wk": L0_attn_Wk,
        "Wugg": L0_attn_Wugg, "Wout": L0_attn_Wout, "Wr": L0_ffn_Wr,
        "br": L0_ffn_br, "W1": L0_ffn_W1, "b1": L0_ffn_b1,
        "W2": L0_ffn_W2, "b2": L0_ffn_b2,
    }
    weights = {k: np.ascontiguousarray(np.asarray(v, dtype=np.float32))
               for k, v in weights.items()}

    nc = _get_program(NCH_FULL)
    in_maps = []
    for b in range(B):
        m = {"x_in": np.ascontiguousarray(x[b])}
        m.update(weights)
        in_maps.append(m)
    res = run_bass_kernel_spmd(nc, in_maps, core_ids=list(range(B)))
    out = np.stack([res.results[b]["y_out"] for b in range(B)], axis=0)
    return out


# version 23 (final)
# speedup vs baseline: 1.0606x; 1.0606x over previous
import contextlib

import numpy as np

import concourse.bass as bass
import concourse.mybir as mybir
from concourse.bass_utils import run_bass_kernel_spmd

F32 = mybir.dt.float32
BF16 = mybir.dt.bfloat16
I32 = mybir.dt.int32
AF = mybir.ActivationFunctionType
ALU = mybir.AluOpType
AX = mybir.AxisListType

D = 1024
S = 4096
STN = 2048
DFF = 4096
NE = 4
T = 512
MT = 4
KC = 8
JCH = 16
JF = 32
NCH_FULL = S // T

LN_EPS = 1e-5
RMS_EPS = 1e-8
HALF_C = float(-0.5 * np.log(512.0))

DMA_SEM_STEP = 16


class Prog:

    SERIAL = ("vector", "scalar")
    OWN = {"vector": "V", "scalar": "Sc"}

    def __init__(self):
        self.ops = []
        self.count = {}
        self.producer = {}
        self.step = {}

    def sem(self, name, producer, step):
        self.count[name] = 0
        self.producer[name] = producer
        self.step[name] = step

    def op(self, engine, fn, waits=(), inc=None):
        w = []
        for t in waits:
            if t is None:
                continue
            s, v = t
            if engine == "tensor" and self.producer[s] == "tensor":
                continue
            w.append((s, v))
        if engine in self.SERIAL:
            own = self.OWN[engine]
            if inc is None:
                inc = own
            prev = self.count[own]
            if prev > 0:
                w.append((own, prev))
        tok = None
        if inc is not None:
            assert (self.producer[inc] == engine
                    or ("tensor" not in (self.producer[inc], engine))), (inc, engine)
            self.count[inc] += self.step[inc]
            tok = (inc, self.count[inc])
        self.ops.append((engine, fn, w, inc))
        return tok

    def emit(self, block, sems):
        for en in ("sync", "gpsimd", "vector", "scalar", "tensor"):
            ops = [o for o in self.ops if o[0] == en]
            if not ops:
                continue

            def body(eng, ops=ops):
                waited = {}
                for (_, fn, waits, inc) in ops:
                    for (s, v) in waits:
                        if v > waited.get(s, 0):
                            eng.wait_ge(sems[s], v)
                            waited[s] = v
                    instr = fn(eng)
                    if inc is not None:
                        instr.then_inc(sems[inc], self.step[inc])

            getattr(block, en)(body)


def build_program(nch=NCH_FULL):
    nc = bass.Bass("TRN2", target_bir_lowering=False)
    seq = nch * T

    x_in = nc.dram_tensor("x_in", [seq, D], F32, kind="ExternalInput").ap()
    ls_in = nc.dram_tensor("ls_scale", [512], F32, kind="ExternalInput").ap()
    rs_in = nc.dram_tensor("rs_scale", [512], F32, kind="ExternalInput").ap()
    sp_in = nc.dram_tensor("split_scale", [D], F32, kind="ExternalInput").ap()
    wk_in = nc.dram_tensor("Wk", [D, STN], F32, kind="ExternalInput").ap()
    wugg_in = nc.dram_tensor("Wugg", [D, 3 * STN], F32, kind="ExternalInput").ap()
    wout_in = nc.dram_tensor("Wout", [STN, D], F32, kind="ExternalInput").ap()
    wr_in = nc.dram_tensor("Wr", [D, NE], F32, kind="ExternalInput").ap()
    br_in = nc.dram_tensor("br", [NE], F32, kind="ExternalInput").ap()
    w1_in = nc.dram_tensor("W1", [NE, D, DFF], F32, kind="ExternalInput").ap()
    b1_in = nc.dram_tensor("b1", [NE, DFF], F32, kind="ExternalInput").ap()
    w2_in = nc.dram_tensor("W2", [NE, DFF, D], F32, kind="ExternalInput").ap()
    b2_in = nc.dram_tensor("b2", [NE, D], F32, kind="ExternalInput").ap()
    y_out = nc.dram_tensor("y_out", [seq, D], F32, kind="ExternalOutput").ap()

    sWA = nc.dram_tensor("sWA", [JCH, 128, KC, 4, 128], BF16, kind="Internal").ap()
    sWout = nc.dram_tensor("sWout", [JCH, 128, D], BF16, kind="Internal").ap()
    sW1 = nc.dram_tensor("sW1", [NE, 8, 128, KC, 4, 128], BF16, kind="Internal").ap()
    sW2 = nc.dram_tensor("sW2", [NE, JF, 128, D], BF16, kind="Internal").ap()

    x_t = x_in.rearrange("(c t p) d -> c t p d", t=MT, p=128)
    y_t = y_out.rearrange("(c t p) d -> c t p d", t=MT, p=128)
    wk_t = wk_in.rearrange("(k p) c -> k p c", p=128)
    wugg_t = wugg_in.rearrange("(k p) c -> k p c", p=128)
    wout_t = wout_in.rearrange("(k p) c -> k p c", p=128)
    w1_t = w1_in.rearrange("e (k p) c -> e k p c", p=128)
    w1pt = w1_in.rearrange("e (k p) c -> e p k c", p=128)
    w2_t = w2_in.rearrange("e (k p) c -> e k p c", p=128)

    p = Prog()
    p.sem("dC", "sync", 16)
    for _s in range(4):
        p.sem(f"dP{_s}", "sync" if _s % 2 == 0 else "gpsimd", 16)
    for _s in range(4):
        p.sem(f"dT{_s}", "scalar", 16)
    for _s in range(4):
        p.sem(f"dZ{_s}", "gpsimd", 16)
    for _s in range(4):
        p.sem(f"dQ{_s}", "sync" if _s % 2 == 0 else "gpsimd", 16)
    for _s in range(4):
        p.sem(f"dU{_s}", "sync" if _s % 2 == 0 else "gpsimd", 16)
    p.sem("dX0", "gpsimd", 16)
    p.sem("dX1", "gpsimd", 16)
    p.sem("dY", "gpsimd", 16)
    p.sem("T", "tensor", 1)
    p.sem("V", "vector", 1)
    p.sem("Sc", "scalar", 1)
    p.sem("G", "gpsimd", 1)

    with contextlib.ExitStack() as stack:
        E = stack.enter_context

        x_sb = E(nc.sbuf_tensor("x_sb", [128, 2, MT, D], F32))
        acc = E(nc.sbuf_tensor("acc", [128, MT, D], F32))
        xn1t = E(nc.sbuf_tensor("xn1t", [128, 2, D], BF16))
        xnf = E(nc.sbuf_tensor("xnf", [128, KC, T], BF16))
        sqd16 = E(nc.sbuf_tensor("sqd16", [128, D], BF16))
        x2hi = E(nc.sbuf_tensor("x2hi", [128, 2, D], BF16))
        x2lo = E(nc.sbuf_tensor("x2lo", [128, 2, D], BF16))
        xnf2h = E(nc.sbuf_tensor("xnf2h", [128, KC, T], BF16))
        xnf2l = E(nc.sbuf_tensor("xnf2l", [128, KC, T], BF16))
        ktile = E(nc.sbuf_tensor("ktile", [128, 2, T + 1], F32))
        tmpgin = E(nc.sbuf_tensor("tmpgin", [128, 2, T], F32))
        u1t = E(nc.sbuf_tensor("u1t", [128, 2, T], F32))
        hneg = E(nc.sbuf_tensor("hneg", [128, 2, T], F32))
        gob = E(nc.sbuf_tensor("gob", [128, 2, T], BF16))
        outs_sb = E(nc.sbuf_tensor("outs_sb", [128, JCH, T], BF16))
        h1_sb = E(nc.sbuf_tensor("h1_sb", [128, JF, T], BF16))
        slab = E(nc.sbuf_tensor("slab", [128, 4, 4096], BF16))
        tmpev = E(nc.sbuf_tensor("tmpev", [128, 2, D], F32))
        stv = E(nc.sbuf_tensor("stv", [128, 96], F32))
        wsb = E(nc.sbuf_tensor("wsb", [128, 16], F32))
        rt4 = E(nc.sbuf_tensor("rt4", [128, 12], F32))
        wT_sb = E(nc.sbuf_tensor("wT_sb", [4, MT, 128], BF16))
        b2f = E(nc.sbuf_tensor("b2f", [4, D], F32))
        b2mv = E(nc.sbuf_tensor("b2mv", [4, D], BF16))
        b1s = E(nc.sbuf_tensor("b1s", [128, 128], F32))
        br_sb = E(nc.sbuf_tensor("br_sb", [128, NE], F32))
        wrh = E(nc.sbuf_tensor("wrh", [128, 32], BF16))
        wrl = E(nc.sbuf_tensor("wrl", [128, 32], BF16))
        wrf = E(nc.sbuf_tensor("wrf", [128, 32], F32))
        sc_sb = E(nc.sbuf_tensor("sc_sb", [128, KC], F32))
        lsrs = E(nc.sbuf_tensor("lsrs", [128, KC], F32))
        spv = E(nc.sbuf_tensor("spv", [128, KC], F32))
        idb = E(nc.sbuf_tensor("idb", [128, 128], BF16))
        idf = E(nc.sbuf_tensor("idf", [128, 128], F32))
        cst = E(nc.sbuf_tensor("cst", [128, 4], F32))
        kcar = E(nc.sbuf_tensor("kcar", [128, JCH], F32))
        hcar = E(nc.sbuf_tensor("hcar", [128, JCH], F32))
        b1v = acc[:, 0, 0:128]
        iii = acc[:, 1, 0:128].bitcast(I32)


        pb = [E(nc.psum_tensor(f"pb{i}", [128, 512], F32)) for i in range(8)]
        pbb = [pb[i][:].bitcast(BF16) for i in range(8)]

        sems = {n: E(nc.semaphore(n)) for n in
                ["dC", "dP0", "dP1", "dP2", "dP3",
                 "dT0", "dT1", "dT2", "dT3", "dZ0", "dZ1", "dZ2", "dZ3",
                 "dQ0", "dQ1", "dQ2", "dQ3", "dU0", "dU1", "dU2", "dU3",
                 "dX0", "dX1", "dY", "T", "V", "Sc", "G"]}

        h1b16 = h1_sb[:].rearrange("p a b -> p (a b)")

        bank_free = {i: None for i in range(8)}

        tG = p.op("gpsimd", lambda e: e.iota(iii, pattern=[[-1, 128]], base=0,
                                             channel_multiplier=1), inc="G")
        tIdb = p.op("vector", lambda e: e.tensor_single_scalar(idb[:], iii, 0.0, ALU.is_equal),
                    waits=[tG], inc="V")
        tIdf = p.op("vector", lambda e: e.tensor_single_scalar(idf[:], iii, 0.0, ALU.is_equal),
                    waits=[tG], inc="V")

        p.op("sync", lambda e: e.dma_start(allow_slow_non_contiguous=True, out=lsrs[:, 0:4],
                                           in_=ls_in.rearrange("(a p) -> p a", p=128)), inc="dC")
        p.op("sync", lambda e: e.dma_start(allow_slow_non_contiguous=True, out=lsrs[:, 4:8],
                                           in_=rs_in.rearrange("(a p) -> p a", p=128)), inc="dC")
        p.op("sync", lambda e: e.dma_start(allow_slow_non_contiguous=True, out=spv[:],
                                           in_=sp_in.rearrange("(a p) -> p a", p=128)), inc="dC")
        p.op("sync", lambda e: e.dma_start(out=br_sb[:],
                                           in_=br_in[None, :].broadcast_to([128, NE])), inc="dC")
        p.op("sync", lambda e: e.dma_start(out=b2f[:], in_=b2_in), inc="dC")
        p.op("sync", lambda e: e.dma_start(allow_slow_non_contiguous=True, out=wrf[:],
                                           in_=wr_in.rearrange("(k p) c -> p k c", p=128)), inc="dC")
        p.op("sync", lambda e: e.dma_start(out=b1v,
                                           in_=b1_in.rearrange("e (x y) -> (e x) y", y=128)), inc="dC")
        tCall = ("dC", p.count["dC"])

        tSc = p.op("vector", lambda e: e.tensor_mul(sc_sb[:], lsrs[:], spv[:]),
                   waits=[tCall], inc="V")
        tBr = tCall
        tB2c = p.op("vector", lambda e: e.tensor_copy(b2mv[:], b2f[:]), waits=[tCall], inc="V")
        tWrh = p.op("vector", lambda e: e.tensor_copy(wrh[:], wrf[:]), waits=[tCall], inc="V")
        tWrl = p.op("vector", lambda e: e.tensor_sub(wrl[:], wrf[:], wrh[:]), inc="V")
        tB1t = p.op("tensor", lambda e: e.transpose(pb[6][:, 0:128], b1v, idf[:]),
                    waits=[tCall, tIdf], inc="T")
        tB1s = p.op("scalar", lambda e: e.copy(b1s[:], pb[6][:, 0:128]), waits=[tB1t], inc="Sc")
        bank_free[6] = tB1s

        p.op("vector", lambda e: e.memset(cst[:, 0:1], LN_EPS), inc="V")
        p.op("vector", lambda e: e.memset(cst[:, 1:2], -0.5), inc="V")
        p.op("vector", lambda e: e.memset(cst[:, 2:3], 0.5), inc="V")
        p.op("vector", lambda e: e.memset(cst[:, 3:4], HALF_C), inc="V")
        p.op("vector", lambda e: e.memset(kcar[:], 1.0), inc="V")
        tHc = p.op("vector", lambda e: e.memset(hcar[:], 0.0), inc="V")

        region_done = {}

        def ensure_region(region):
            return [t for t in region_done[region] if t is not None]

        slot_free = [None, None, None, None]
        slot_rr = [0]

        def load_piece(in_ap, region):
            rt = ensure_region(region)
            s = slot_rr[0]
            slot_rr[0] = (s + 1) % 4
            eng = "sync" if s % 2 == 0 else "gpsimd"
            tok = p.op(eng, lambda e, s=s, in_ap=in_ap: e.dma_start(out=slab[:, s, :], in_=in_ap),
                       waits=[slot_free[s], store_free[s]] + list(rt), inc=f"dP{s}")
            return s, tok

        store_free = [None, None, None, None]

        def load_piece_cast(f32_ap, store_ap, region):
            s = slot_rr[0]
            slot_rr[0] = (s + 1) % 4
            tokL = p.op("gpsimd",
                        lambda e, s=s, f32_ap=f32_ap: e.dma_start(out=slab[:, s, :], in_=f32_ap),
                        waits=[slot_free[s], store_free[s]], inc=f"dZ{s}")
            tokS = p.op("scalar",
                        lambda e, s=s, store_ap=store_ap: e.dma_start(out=store_ap, in_=slab[:, s, :]),
                        waits=[tokL], inc=f"dT{s}")
            store_free[s] = tokS
            region_done.setdefault(region, [None] * 4)[s] = tokS
            return s, tokL

        wk_p = wk_in.rearrange("(k p) c -> p k c", p=128)
        wugg_p = wugg_in.rearrange("(k p) (w jc) -> p k w jc", p=128, w=3)

        def load_piece_cast_wa(j):
            s = slot_rr[0]
            slot_rr[0] = (s + 1) % 4
            slab4 = slab[:, s, :].rearrange("p (k w c) -> p k w c", w=4, c=128)
            p.op("gpsimd",
                 lambda e, slab4=slab4, j=j: e.dma_start(
                     out=slab4[:, :, 0, :], in_=wk_p[:, :, j * 128:(j + 1) * 128]),
                 waits=[slot_free[s], store_free[s]], inc=f"dZ{s}")
            tokL = None
            for w in range(3):
                tokL = p.op("gpsimd",
                            lambda e, slab4=slab4, j=j, w=w: e.dma_start(
                                out=slab4[:, :, w + 1, :],
                                in_=wugg_p[:, :, w, j * 128:(j + 1) * 128]),
                            inc=f"dZ{s}")
            region_done.setdefault("WA", [None] * 4)
            return s, tokL

        def free_slot(s, tok):
            assert tok is not None
            slot_free[s] = tok

        x_ready = [None] * nch
        chunk_done = [None] * nch
        acc_free = None
        attn_prev_v = [None, None]
        tr1_prev = [None, None]
        tr2_prev = [None, None]
        sc_stat_prev = [None, None]
        ln_v_prev = [None, None]
        tex_prev = [None]
        scale_prev = [None] * MT
        tw_m = [None] * MT

        SW = {}
        for i, name in enumerate(["Sl", "Sr", "Ql", "Qr", "Ss", "m", "E2", "msq",
                                  "var", "r", "nl", "nr", "s2l", "s2r", "t1", "t2",
                                  "mx", "mxn", "s2", "rs",
                                  "g01", "g02", "g03", "g12", "g13", "g23",
                                  "rk0", "rk1", "rk2", "rk3", "gl", "gr", "lt",
                                  "m0", "m1", "m2", "m3", "r0", "r1", "r2", "r3",
                                  "aa", "bb"]):
            SW[name] = i

        def sv(name, par):
            c = 2 * SW[name] + par
            return stv[:, c:c + 1]

        pre_ln = {}
        ln1_tap = [None] * MT
        h1_free = [None]

        def emit_ln1_tile(ch, mt):
            cb = ch % 2
            par = mt % 2
            xtile = x_sb[:, cb, mt, :]
            xl = xtile[:, 0:512]
            xr = xtile[:, 512:1024]
            p.op("scalar", lambda e, xl=xl, par=par:
                 e.activation(sqd16[:, 0:512], xl, AF.Square, accum_out=sv("Ql", par)),
                 waits=[x_ready[ch], ln_v_prev[par]])
            tsq = p.op("scalar", lambda e, xr=xr, par=par:
                       e.activation(sqd16[:, 512:1024], xr, AF.Square, accum_out=sv("Qr", par)),
                       inc="Sc")
            p.op("vector", lambda e, xl=xl, par=par: e.reduce_sum(sv("Sl", par), xl, axis=AX.X),
                 waits=[x_ready[ch], sc_stat_prev[par]])
            p.op("vector", lambda e, xr=xr, par=par: e.reduce_sum(sv("Sr", par), xr, axis=AX.X))
            p.op("vector", lambda e, par=par: e.tensor_add(sv("Ss", par), sv("Sl", par), sv("Sr", par)))
            p.op("vector", lambda e, par=par, mt=mt: e.tensor_single_scalar(sv("m" + str(mt), 0), sv("Ss", par), 1.0 / D, ALU.mult))
            p.op("vector", lambda e, par=par: e.tensor_add(sv("E2", par), sv("Ql", par), sv("Qr", par)),
                 waits=[tsq])
            p.op("vector", lambda e, par=par: e.tensor_single_scalar(sv("E2", par), sv("E2", par), 1.0 / D, ALU.mult))
            p.op("vector", lambda e, par=par, mt=mt: e.tensor_mul(sv("msq", par), sv("m" + str(mt), 0), sv("m" + str(mt), 0)))
            tv = p.op("vector", lambda e, par=par: e.tensor_sub(sv("var", par), sv("E2", par), sv("msq", par)), inc="V")
            ln_v_prev[par] = tv
            p.op("scalar", lambda e, par=par: e.activation(sv("lt", par), sv("var", par), AF.Ln, bias=cst[:, 0:1]),
                 waits=[tv])
            tr_ = p.op("scalar", lambda e, par=par, mt=mt: e.activation(sv("r" + str(mt), 0), sv("lt", par), AF.Exp, scale=cst[:, 1:2]),
                       inc="Sc")
            tq = None
            for (qn, sn, s2n) in (("Ql", "Sl", "s2l"), ("Qr", "Sr", "s2r")):
                p.op("vector", lambda e, par=par, sn=sn, mt=mt:
                     e.scalar_tensor_tensor(sv("t1", par), sv("m" + str(mt), 0), -2.0, sv(sn, par), ALU.mult, ALU.mult))
                p.op("vector", lambda e, par=par, mt=mt:
                     e.scalar_tensor_tensor(sv("t2", par), sv("m" + str(mt), 0), 512.0, sv("m" + str(mt), 0), ALU.mult, ALU.mult))
                p.op("vector", lambda e, par=par, qn=qn, s2n=s2n:
                     e.tensor_add(sv(s2n, par), sv("t1", par), sv(qn, par)))
                tq = p.op("vector", lambda e, par=par, s2n=s2n:
                          e.tensor_add(sv(s2n, par), sv(s2n, par), sv("t2", par)), inc="V")
            p.op("scalar", lambda e, par=par: e.activation(sv("nl", par), sv("s2l", par), AF.Ln), waits=[tq])
            p.op("scalar", lambda e, par=par: e.activation(sv("nl", par), sv("nl", par), AF.Exp, scale=cst[:, 2:3], bias=cst[:, 3:4]))
            p.op("scalar", lambda e, par=par: e.activation(sv("nr", par), sv("s2r", par), AF.Ln))
            ts2 = p.op("scalar", lambda e, par=par: e.activation(sv("nr", par), sv("nr", par), AF.Exp, scale=cst[:, 2:3], bias=cst[:, 3:4]),
                       inc="Sc")
            sc_stat_prev[par] = ts2
            for (nn, gn) in (("nl", "gl"), ("nr", "gr")):
                p.op("vector", lambda e, par=par, nn=nn, mt=mt: e.tensor_mul(sv(nn, par), sv(nn, par), sv("r" + str(mt), 0)),
                     waits=[tr_, ts2])
                p.op("vector", lambda e, par=par, nn=nn: e.tensor_single_scalar(sv(nn, par), sv(nn, par), RMS_EPS, ALU.add))
                p.op("vector", lambda e, par=par, nn=nn: e.reciprocal(sv(nn, par), sv(nn, par)))
                p.op("vector", lambda e, par=par, nn=nn, gn=gn, mt=mt: e.tensor_mul(sv(gn, par), sv(nn, par), sv("r" + str(mt), 0)))
            p.op("vector", lambda e, xl=xl, par=par, mt=mt:
                 e.tensor_scalar(xn1t[:, par, 0:512], xl, sv("m" + str(mt), 0), sv("gl", par), ALU.subtract, ALU.mult),
                 waits=[tr1_prev[par]])
            tap = p.op("vector", lambda e, xr=xr, par=par, mt=mt:
                       e.tensor_scalar(xn1t[:, par, 512:1024], xr, sv("m" + str(mt), 0), sv("gr", par),
                                       ALU.subtract, ALU.mult), inc="V")
            p.op("vector", lambda e, par=par, mt=mt:
                 e.tensor_single_scalar(sv("aa", par), sv("r" + str(mt), 0), 1.0, ALU.add))
            p.op("vector", lambda e, par=par, mt=mt:
                 e.tensor_mul(sv("bb", par), sv("m" + str(mt), 0), sv("r" + str(mt), 0)))
            p.op("vector", lambda e, xtile=xtile, par=par:
                 e.tensor_scalar(xtile, xtile, sv("aa", par), sv("bb", par), ALU.mult, ALU.subtract))
            return tap

        def emit_ln1_tr(mt, tap, scaled=False):
            par = mt % 2
            ev = None
            for half in range(2):
                bk = 6 + half
                tk = None
                for q in range(4):
                    kc = half * 4 + q
                    tk = p.op("tensor",
                              lambda e, par=par, kc=kc, q=q, bk=bk:
                              e.transpose(pbb[bk][:, q * 128:(q + 1) * 128],
                                          xn1t[:, par, kc * 128:(kc + 1) * 128], idb[:]),
                              waits=[tap, tIdb, bank_free[bk]] if q == 0 else [],
                              inc="T" if q == 3 else None)
                if scaled:
                    for q in range(4):
                        kc = half * 4 + q
                        ev = p.op("scalar",
                                  lambda e, kc=kc, q=q, mt=mt, bk=bk:
                                  e.activation(xnf[:, kc, mt * 128:(mt + 1) * 128],
                                               pbb[bk][:, q * 128:(q + 1) * 128],
                                               AF.Copy, scale=sc_sb[:, kc:kc + 1]),
                                  waits=[tk, tSc] if q == 0 else [], inc="Sc")
                else:
                    ev = p.op("scalar",
                              lambda e, half=half, mt=mt, bk=bk:
                              e.copy(xnf[:, half * 4:half * 4 + 4, mt * 128:(mt + 1) * 128],
                                     pbb[bk][:, 0:512].rearrange("p (q c) -> p q c", q=4)),
                              waits=[tk], inc="Sc")
                bank_free[bk] = ev
            tr1_prev[par] = tk
            return ev

        def load_x(ch, waits):
            buf = ch % 2
            toks = []
            for mt in range(MT):
                toks.append(p.op("gpsimd",
                                 lambda e, ch=ch, mt=mt, buf=buf:
                                 e.dma_start(out=x_sb[:, buf, mt, :], in_=x_t[ch, mt]),
                                 waits=waits, inc=f"dX{buf}"))
            x_ready[ch] = toks[-1]

        load_x(0, [])
        if nch > 1:
            load_x(1, [])

        last_store = None

        for ch in range(nch):
            cb = ch % 2

            def xs(mt, cb=cb):
                return x_sb[:, cb, mt, :]

            if ch in pre_ln:
                xnf_ready = pre_ln.pop(ch)
            else:
                xnf_ready = None
                for mt in range(MT):
                    xnf_ready = emit_ln1_tr(mt, emit_ln1_tile(ch, mt), scaled=(ch == 0))

            wa_h1_tok = [None] * 4
            if ch > 0:
                rtWA = ensure_region("WA")
                for j4 in range(4):
                    wa_h1_tok[j4] = p.op(
                        "sync" if j4 % 2 == 0 else "gpsimd",
                        lambda e, j4=j4: e.dma_start(
                            out=h1b16[:, j4 * 4096:(j4 + 1) * 4096],
                            in_=sWA[j4].rearrange("p k w c -> p (k w c)")),
                        waits=[h1_free[0]] + list(rtWA), inc=f"dQ{j4}")

            attn_ev_s = [None] * JCH
            attn_ev_v = [None] * JCH
            tmm_j3 = None
            wo_slots = []
            use_h1_wo = (ch > 0)
            for j in range(JCH):
                jb = j % 2
                pbase = 4 - jb * 4
                if ch > 0 and j < 4:
                    sl, tokL = None, wa_h1_tok[j]
                elif ch == 0:
                    sl, tokL = load_piece_cast_wa(j)
                else:
                    sl, tokL = load_piece(sWA[j].rearrange("p k w c -> p (k w c)"), "WA")
                if use_h1_wo and j == 10:
                    rtWO = ensure_region("WO")
                    for g in range(4):
                        tokWO = p.op(
                            "sync" if g % 2 == 0 else "gpsimd",
                            lambda e, g=g: e.dma_start(
                                out=h1b16[:, g * 4096:(g + 1) * 4096],
                                in_=sWout[4 * g:4 * g + 4].transpose([1, 0, 2])),
                            waits=list(rtWO) + [tmm_j3], inc=f"dU{g}")
                        wo_slots.append((None, tokWO))
                waits = [xnf_ready, tokL]
                if j >= 2:
                    waits += [attn_ev_s[j - 2], attn_ev_v[j - 2]]
                else:
                    waits += [bank_free[pbase + i] for i in range(4)]
                tmm = None
                for w in range(4):
                    for kc in range(KC):
                        if sl is None:
                            lhs_ap = h1b16[:, j * 4096 + (kc * 4 + w) * 128:
                                           j * 4096 + (kc * 4 + w) * 128 + 128]
                        else:
                            lhs_ap = slab[:, sl, (kc * 4 + w) * 128:(kc * 4 + w) * 128 + 128]
                        tmm = p.op("tensor",
                                   lambda e, lhs_ap=lhs_ap, w=w, kc=kc, pbase=pbase:
                                   e.matmul(pb[pbase + w][:, 0:512],
                                            lhsT=lhs_ap,
                                            rhs=xnf[:, kc, :],
                                            start=(kc == 0), stop=(kc == KC - 1)),
                                   waits=waits if (w == 0 and kc == 0) else [],
                                   inc="T" if (w == 3 and kc == KC - 1) else None)
                if j == 3:
                    tmm_j3 = tmm
                if sl is not None:
                    free_slot(sl, tmm)
                if ch == 0:
                    tsc_ = None
                    for kc in range(KC):
                        tsc_ = p.op("vector",
                                    lambda e, sl=sl, kc=kc:
                                    e.tensor_scalar_mul(slab[:, sl, kc * 512:(kc + 1) * 512],
                                                        slab[:, sl, kc * 512:(kc + 1) * 512],
                                                        sc_sb[:, kc:kc + 1]),
                                    waits=[tmm, tSc] if kc == 0 else [], inc="V")
                    tokS = p.op("scalar",
                                lambda e, sl=sl, j=j: e.dma_start(
                                    out=sWA[j].rearrange("p k w c -> p (k w c)"),
                                    in_=slab[:, sl, :]),
                                waits=[tsc_], inc=f"dT{sl}")
                    store_free[sl] = tokS
                    region_done["WA"][sl] = tokS
                swait = [tmm]
                if j >= 2:
                    swait.append(attn_ev_v[j - 2])
                else:
                    swait.append(attn_prev_v[j])
                p.op("scalar", lambda e, jb=jb, pbase=pbase:
                     e.activation(ktile[:, jb, 1:T + 1], pb[pbase + 0][:, 0:512], AF.Sigmoid),
                     waits=swait)
                p.op("scalar", lambda e, jb=jb, pbase=pbase:
                     e.activation(tmpgin[:, jb, :], pb[pbase + 2][:, 0:512], AF.Sigmoid))
                tse = p.op("scalar", lambda e, jb=jb, pbase=pbase:
                           e.activation(gob[:, jb, :], pb[pbase + 3][:, 0:512], AF.Sigmoid), inc="Sc")
                attn_ev_s[j] = tse
                p.op("vector", lambda e, jb=jb, j=j: e.tensor_copy(ktile[:, jb, 0:1], kcar[:, j:j + 1]),
                     waits=[tse, tHc])
                p.op("vector", lambda e, jb=jb, pbase=pbase:
                     e.tensor_mul(u1t[:, jb, :], pb[pbase + 1][:, 0:512], tmpgin[:, jb, :]))
                p.op("vector", lambda e, jb=jb:
                     e.scalar_tensor_tensor(u1t[:, jb, :], ktile[:, jb, 1:T + 1], 1.0, u1t[:, jb, :],
                                            ALU.subtract, ALU.mult))
                p.op("vector", lambda e, jb=jb, j=j:
                     e.tensor_tensor_scan(hneg[:, jb, :], ktile[:, jb, 0:T], u1t[:, jb, :],
                                          hcar[:, j:j + 1], ALU.mult, ALU.add))
                p.op("vector", lambda e, jb=jb, j=j: e.tensor_copy(hcar[:, j:j + 1], hneg[:, jb, T - 1:T]))
                p.op("vector", lambda e, jb=jb, j=j: e.tensor_copy(kcar[:, j:j + 1], ktile[:, jb, T:T + 1]))
                attn_ev_v[j] = p.op("vector", lambda e, jb=jb, j=j:
                                    e.scalar_tensor_tensor(outs_sb[:, j, :], hneg[:, jb, :], -1.0, gob[:, jb, :],
                                                           ALU.mult, ALU.mult), inc="V")
            attn_prev_v = [attn_ev_v[JCH - 2], attn_ev_v[JCH - 1]]
            for b in range(8):
                bank_free[b] = attn_ev_v[JCH - 1] if (b % 4) == 1 else attn_ev_s[JCH - 1]
            bank_free[1] = attn_ev_v[JCH - 1]
            bank_free[5] = attn_ev_v[JCH - 1]

            if not use_h1_wo:
                for g in range(4):
                    sl, tokL = load_piece_cast(
                        wout_t[4 * g:4 * g + 4].transpose([1, 0, 2]),
                        sWout[4 * g:4 * g + 4].transpose([1, 0, 2]), "WO")
                    wo_slots.append((sl, tokL))
            xp_tok = [None] * MT
            xnf2_ready = [None] * MT
            tmm4 = None

            def emit_tr5(mt, thl):
                par = mt % 2
                ev = None
                tk = None
                for (src, dst) in ((x2hi, xnf2h), (x2lo, xnf2l)):
                    for half in range(2):
                        bk = 6 + half
                        for q in range(4):
                            kc = half * 4 + q
                            tk = p.op("tensor",
                                      lambda e, par=par, kc=kc, q=q, bk=bk, src=src:
                                      e.transpose(pbb[bk][:, q * 128:(q + 1) * 128],
                                                  src[:, par, kc * 128:(kc + 1) * 128], idb[:]),
                                      waits=[thl, bank_free[bk]] if q == 0 else [],
                                      inc="T" if q == 3 else None)
                        ev = p.op("scalar",
                                  lambda e, half=half, mt=mt, bk=bk, dst=dst:
                                  e.copy(dst[:, half * 4:half * 4 + 4, mt * 128:(mt + 1) * 128],
                                         pbb[bk][:, 0:512].rearrange("p (q c) -> p q c", q=4)),
                                  waits=[tk], inc="Sc")
                        bank_free[bk] = ev
                xnf2_ready[mt] = ev
                tr2_prev[par] = tk

            thl_m = [None] * MT
            for m in range(MT):
                par = m % 2
                pb4 = (m % 2) * 2
                for k in range(JCH):
                    sl, tokL = wo_slots[k // 4]
                    kk = k % 4
                    for h in range(2):
                        w = []
                        if k == 0 and h == 0:
                            if m == 0:
                                w = [attn_ev_v[JCH - 2], attn_ev_s[JCH - 2],
                                     wo_slots[0][1]]
                            w += [bank_free[pb4], bank_free[pb4 + 1]]
                        elif m == 0 and kk == 0 and h == 0:
                            w = [wo_slots[k // 4][1]]
                        if m == 0 and k == JCH - 1 and h == 0:
                            w = w + [attn_ev_v[JCH - 1]]
                        if use_h1_wo:
                            g_ = k // 4
                            rhs_ap = h1b16[:, g_ * 4096 + kk * 1024 + h * 512:
                                           g_ * 4096 + kk * 1024 + h * 512 + 512]
                        else:
                            rhs_ap = slab[:, sl, kk * 1024 + h * 512: kk * 1024 + h * 512 + 512]
                        tmm4 = p.op("tensor",
                                    lambda e, rhs_ap=rhs_ap, k=k, m=m, h=h:
                                    e.matmul(pb[(0 if m % 2 == 0 else 2) + h][:, 0:512],
                                             lhsT=outs_sb[:, k, m * 128:(m + 1) * 128],
                                             rhs=rhs_ap,
                                             start=(k == 0), stop=(k == JCH - 1)),
                                    waits=w,
                                    inc="T" if (k == JCH - 1 and h == 1) else None)
                p.op("vector", lambda e, m=m, cb=cb, pb4=pb4:
                     e.tensor_add(x_sb[:, cb, m, 0:512], x_sb[:, cb, m, 0:512], pb[pb4][:, 0:512]),
                     waits=[tmm4])
                xp_tok[m] = p.op("vector", lambda e, m=m, cb=cb, pb4=pb4:
                                 e.tensor_add(x_sb[:, cb, m, 512:1024], x_sb[:, cb, m, 512:1024],
                                              pb[pb4 + 1][:, 0:512]), inc="V")
                bank_free[pb4] = xp_tok[m]
                bank_free[pb4 + 1] = xp_tok[m]
                xt_ = xs(m)
                tsq5 = p.op("scalar", lambda e, xt_=xt_, par=par:
                            e.activation(sqd16[:], xt_, AF.Square, accum_out=sv("Ql", par)),
                            waits=[xp_tok[m], ln_v_prev[par]], inc="Sc")
                p.op("vector", lambda e, xt_=xt_, par=par: e.reduce_sum(sv("Ss", par), xt_, axis=AX.X),
                     waits=[xp_tok[m], sc_stat_prev[par]])
                p.op("vector", lambda e, par=par: e.tensor_single_scalar(sv("m", par), sv("Ss", par), 1.0 / D, ALU.mult))
                p.op("vector", lambda e, par=par: e.tensor_single_scalar(sv("E2", par), sv("Ql", par), 1.0 / D, ALU.mult),
                     waits=[tsq5])
                p.op("vector", lambda e, par=par: e.tensor_mul(sv("msq", par), sv("m", par), sv("m", par)))
                tv = p.op("vector", lambda e, par=par: e.tensor_sub(sv("var", par), sv("E2", par), sv("msq", par)), inc="V")
                ln_v_prev[par] = tv
                p.op("scalar", lambda e, par=par: e.activation(sv("lt", par), sv("var", par), AF.Ln, bias=cst[:, 0:1]),
                     waits=[tv])
                tr_ = p.op("scalar", lambda e, par=par: e.activation(sv("r", par), sv("lt", par), AF.Exp, scale=cst[:, 1:2]),
                           inc="Sc")
                sc_stat_prev[par] = tr_
                p.op("vector", lambda e, xt_=xt_, par=par:
                     e.tensor_scalar(tmpev[:, par, :], xt_, sv("m", par), sv("r", par), ALU.subtract, ALU.mult),
                     waits=[tr_, tr2_prev[par]])
                p.op("vector", lambda e, par=par: e.tensor_copy(x2hi[:, par, :], tmpev[:, par, :]))
                thl_m[m] = p.op("vector", lambda e, par=par: e.tensor_sub(x2lo[:, par, :], tmpev[:, par, :], x2hi[:, par, :]),
                                inc="V")
                if m >= 1:
                    emit_tr5(m - 1, thl_m[m - 1])
            for sl, _ in wo_slots:
                if sl is not None:
                    free_slot(sl, tmm4)
            p6a_pending = True

            racc_tok = [None] * MT

            def p6a(m):
                combos = [(xnf2h, wrh), (xnf2l, wrh), (xnf2h, wrl)]
                tmm6 = None
                for ci, (xa, wb) in enumerate(combos):
                    for kc in range(KC):
                        first = (ci == 0 and kc == 0)
                        last = (ci == 2 and kc == KC - 1)
                        w = [xnf2_ready[m], bank_free[0], tWrh, tWrl] if first else []
                        tmm6 = p.op("tensor",
                                    lambda e, xa=xa, wb=wb, kc=kc, m=m, first=first, last=last:
                                    e.matmul(pb[0][:, 0:4],
                                             lhsT=xa[:, kc, m * 128:(m + 1) * 128],
                                             rhs=wb[:, kc * 4:kc * 4 + 4],
                                             start=first, stop=last),
                                    waits=w,
                                    inc="T" if last else None)
                lg = rt4[:, 0:4]
                ex = rt4[:, 4:8]
                mk = rt4[:, 8:12]
                tlg = p.op("vector", lambda e, lg=lg: e.tensor_add(lg, pb[0][:, 0:4], br_sb[:]),
                           waits=[tmm6, tBr, tex_prev[0]], inc="V")
                bank_free[0] = tlg
                p.op("vector", lambda e, lg=lg: e.reduce_max(sv("mx", 0), lg, axis=AX.X))
                tmx = p.op("vector", lambda e: e.tensor_single_scalar(sv("mxn", 0), sv("mx", 0), -1.0, ALU.mult),
                           inc="V")
                tex = p.op("scalar", lambda e, lg=lg, ex=ex: e.activation(ex, lg, AF.Exp, bias=sv("mxn", 0)),
                           waits=[tmx], inc="Sc")
                tex_prev[0] = tex
                for (a, b, gn) in ((0, 1, "g01"), (0, 2, "g02"), (0, 3, "g03"),
                                   (1, 2, "g12"), (1, 3, "g13"), (2, 3, "g23")):
                    p.op("vector", lambda e, lg=lg, a=a, b=b, gn=gn:
                         e.tensor_tensor(sv(gn, 0), lg[:, a:a + 1], lg[:, b:b + 1], ALU.is_ge))
                p.op("vector", lambda e: e.tensor_add(sv("rk0", 0), sv("g01", 0), sv("g02", 0)))
                p.op("vector", lambda e: e.tensor_add(sv("rk0", 0), sv("rk0", 0), sv("g03", 0)))
                p.op("vector", lambda e: e.tensor_single_scalar(sv("rk0", 0), sv("rk0", 0), -1.0, ALU.mult))
                p.op("vector", lambda e: e.tensor_single_scalar(sv("rk0", 0), sv("rk0", 0), 3.0, ALU.add))
                p.op("vector", lambda e: e.tensor_sub(sv("rk1", 0), sv("g12", 0), sv("g01", 0)))
                p.op("vector", lambda e: e.tensor_add(sv("rk1", 0), sv("rk1", 0), sv("g13", 0)))
                p.op("vector", lambda e: e.tensor_single_scalar(sv("rk1", 0), sv("rk1", 0), -1.0, ALU.mult))
                p.op("vector", lambda e: e.tensor_single_scalar(sv("rk1", 0), sv("rk1", 0), 2.0, ALU.add))
                p.op("vector", lambda e: e.tensor_add(sv("rk2", 0), sv("g02", 0), sv("g12", 0)))
                p.op("vector", lambda e: e.tensor_sub(sv("rk2", 0), sv("rk2", 0), sv("g23", 0)))
                p.op("vector", lambda e: e.tensor_single_scalar(sv("rk2", 0), sv("rk2", 0), 1.0, ALU.add))
                p.op("vector", lambda e: e.tensor_add(sv("rk3", 0), sv("g03", 0), sv("g13", 0)))
                p.op("vector", lambda e: e.tensor_add(sv("rk3", 0), sv("rk3", 0), sv("g23", 0)))
                for i_, rk in enumerate(("rk0", "rk1", "rk2", "rk3")):
                    p.op("vector", lambda e, mk=mk, i_=i_, rk=rk:
                         e.tensor_single_scalar(mk[:, i_:i_ + 1], sv(rk, 0), 1.5, ALU.is_lt))
                p.op("vector", lambda e, mk=mk, ex=ex: e.tensor_mul(mk, mk, ex), waits=[tex])
                p.op("vector", lambda e, mk=mk: e.reduce_sum(sv("s2", 0), mk, axis=AX.X))
                p.op("vector", lambda e: e.reciprocal(sv("rs", 0), sv("s2", 0)))
                tw_m[m] = p.op("vector", lambda e, mk=mk, m=m:
                               e.tensor_scalar_mul(wsb[:, m * 4:m * 4 + 4], mk, sv("rs", 0)),
                               waits=[scale_prev[m]], inc="V")

            def p6b():
                for m in range(MT):
                    twt = p.op("tensor", lambda e, m=m: e.transpose(pb[1][0:4, 0:128], wsb[:, m * 4:m * 4 + 4], idf[:]),
                               waits=[tw_m[m], bank_free[1], tIdf], inc="T")
                    twte = p.op("vector", lambda e, m=m: e.tensor_copy(wT_sb[0:4, m, :], pb[1][0:4, 0:128]),
                                waits=[twt], inc="V")
                    bank_free[1] = twte
                    tb2 = [None, None]
                    for h in range(2):
                        tb2[h] = p.op("tensor", lambda e, m=m, h=h:
                                      e.matmul(pb[2 + h][:, 0:512], lhsT=wT_sb[0:4, m, :],
                                               rhs=b2mv[0:4, h * 512:h * 512 + 512], start=True, stop=True),
                                      waits=[twte, bank_free[2 + h], tB2c], inc="T")
                    aw = [tb2[0], tb2[1]]
                    if acc_free is not None:
                        aw.append(acc_free)
                    p.op("vector", lambda e, m=m, cb=cb:
                         e.tensor_add(acc[:, m, 0:512], x_sb[:, cb, m, 0:512], pb[2][:, 0:512]),
                         waits=aw)
                    racc_tok[m] = p.op("vector", lambda e, m=m, cb=cb:
                                       e.tensor_add(acc[:, m, 512:1024], x_sb[:, cb, m, 512:1024], pb[3][:, 0:512]),
                                       inc="V")
                    bank_free[2] = racc_tok[m]
                    bank_free[3] = racc_tok[m]
                chunk_done[ch] = racc_tok[MT - 1]
                if ch + 2 < nch:
                    load_x(ch + 2, [chunk_done[ch]])

            for m in range(MT - 1):
                p6a(m)
            emit_tr5(MT - 1, thl_m[MT - 1])
            p6a(MT - 1)

            gel_tok = [None] * JF
            prev_ev = [None] * MT
            for e_ in range(NE):
                for jg in range(8):
                    if ch == 0:
                        sl, tokL = load_piece_cast(
                            w1pt[e_][:, :, jg * 512:(jg + 1) * 512],
                            sW1[e_, jg].rearrange("p k jp c -> p (k jp c)"), f"W1{e_}")
                    else:
                        sl, tokL = load_piece(sW1[e_, jg].rearrange("p k jp c -> p (k jp c)"), f"W1{e_}")
                    tmm7 = None
                    for jp in range(4):
                        j = jg * 4 + jp
                        bk = 4 + j % 2
                        w = [tokL, xnf2_ready[MT - 1], bank_free[bk]]
                        if j >= 2:
                            w.append(gel_tok[j - 2])
                        for kc in range(KC):
                            tmm7 = p.op("tensor",
                                        lambda e, sl=sl, kc=kc, jp=jp, bk=bk:
                                        e.matmul(pb[bk][:, 0:512],
                                                 lhsT=slab[:, sl, (kc * 4 + jp) * 128:(kc * 4 + jp) * 128 + 128],
                                                 rhs=xnf2h[:, kc, :],
                                                 start=(kc == 0), stop=(kc == KC - 1)),
                                        waits=w if kc == 0 else [],
                                        inc="T" if kc == KC - 1 else None)
                        gw = [tmm7, tB1s]
                        gel_tok[j] = p.op("scalar",
                                          lambda e, j=j, e_=e_, bk=bk:
                                          e.activation(h1_sb[:, j, :], pb[bk][:, 0:512], AF.Gelu,
                                                       bias=b1s[:, e_ * 32 + j:e_ * 32 + j + 1]),
                                          waits=gw, inc="Sc")
                        bank_free[bk] = gel_tok[j]
                    free_slot(sl, tmm7)
                if e_ == 0:
                    p6b()
                if ch + 1 < nch:
                    sched = {0: (("t", 0), ("t", 1)),
                             1: (("r", 0), ("t", 2), ("r", 1)),
                             2: (("t", 3), ("r", 2)),
                             3: (("r", 3),)}
                    for kind, mt_ in sched.get(e_, ()):
                        if kind == "t":
                            ln1_tap[mt_] = emit_ln1_tile(ch + 1, mt_)
                        else:
                            pre_ln[ch + 1] = emit_ln1_tr(mt_, ln1_tap[mt_])
                tmm8 = None
                for g in range(8):
                    if ch == 0:
                        sl, tokL = load_piece_cast(
                            w2_t[e_, 4 * g:4 * g + 4].transpose([1, 0, 2]),
                            sW2[e_, 4 * g:4 * g + 4].transpose([1, 0, 2]), f"W2{e_}")
                    else:
                        sl, tokL = load_piece(sW2[e_, 4 * g:4 * g + 4].transpose([1, 0, 2]), f"W2{e_}")
                    for jj in range(4):
                        j = 4 * g + jj
                        for m in range(MT):
                            for h in range(2):
                                w = []
                                if jj == 0 and m == 0 and h == 0:
                                    w = [tokL, gel_tok[JF - 1] if g > 0 else gel_tok[j]]
                                    if g == 0:
                                        w += [bank_free[b] for b in range(8)]
                                        w += [t for t in prev_ev if t is not None]
                                elif m == 0 and h == 0:
                                    w = [gel_tok[j]] if g == 0 else []
                                tmm8 = p.op("tensor",
                                            lambda e, sl=sl, j=j, jj=jj, m=m, h=h:
                                            e.matmul(pb[m * 2 + h][:, 0:512],
                                                     lhsT=h1_sb[:, j, m * 128:(m + 1) * 128],
                                                     rhs=slab[:, sl, jj * 1024 + h * 512:jj * 1024 + h * 512 + 512],
                                                     start=(j == 0), stop=(j == JF - 1)),
                                            waits=w,
                                            inc="T" if (jj == 3 and m == MT - 1 and h == 1) else None)
                    free_slot(sl, tmm8)
                last_tad = {0: None, 1: None}
                for m in ((3, 2, 0, 1) if e_ == NE - 1 else (2, 3, 0, 1)):
                    par = m % 2
                    tsc_w = [tmm8, tw_m[m], last_tad[par]]
                    if prev_ev[m] is not None:
                        tsc_w.append(prev_ev[m])
                    p.op("scalar", lambda e, m=m, e_=e_, par=par:
                         e.activation(tmpev[:, par, 0:512], pb[m * 2][:, 0:512], AF.Copy,
                                      scale=wsb[:, m * 4 + e_:m * 4 + e_ + 1]),
                         waits=tsc_w)
                    tsc1 = p.op("scalar", lambda e, m=m, e_=e_, par=par:
                                e.activation(tmpev[:, par, 512:1024], pb[m * 2 + 1][:, 0:512], AF.Copy,
                                             scale=wsb[:, m * 4 + e_:m * 4 + e_ + 1]),
                                inc="Sc")
                    scale_prev[m] = tsc1
                    tad = p.op("vector", lambda e, m=m, par=par:
                               e.tensor_add(acc[:, m, :], acc[:, m, :], tmpev[:, par, :]),
                               waits=[tsc1, racc_tok[m]], inc="V")
                    prev_ev[m] = tad
                    last_tad[par] = tad
                    bank_free[m * 2] = tad
                    bank_free[m * 2 + 1] = tad

            h1_free[0] = tmm8

            st = None
            for mt in range(MT):
                st = p.op("gpsimd", lambda e, ch=ch, mt=mt:
                          e.dma_start(out=y_t[ch, mt], in_=acc[:, mt, :]),
                          waits=[prev_ev[mt]], inc="dY")
            acc_free = st
            last_store = st

        p.op("gpsimd", lambda e: e.nop(), waits=[last_store])

        with nc.Block() as block:
            p.emit(block, sems)

    return nc


_CACHE = {}


def _get_program(nch):
    if nch not in _CACHE:
        _CACHE[nch] = build_program(nch)
    return _CACHE[nch]


def kernel(L0_attn_x, L0_attn_ls_scale, L0_attn_rs_scale, L0_attn_split_scale,
           L0_attn_Wk, L0_attn_Wugg, L0_attn_Wout,
           L0_ffn_Wr, L0_ffn_br, L0_ffn_W1, L0_ffn_b1, L0_ffn_W2, L0_ffn_b2):
    x = np.ascontiguousarray(np.asarray(L0_attn_x, dtype=np.float32))
    B = x.shape[0]
    weights = {
        "ls_scale": L0_attn_ls_scale, "rs_scale": L0_attn_rs_scale,
        "split_scale": L0_attn_split_scale, "Wk": L0_attn_Wk,
        "Wugg": L0_attn_Wugg, "Wout": L0_attn_Wout, "Wr": L0_ffn_Wr,
        "br": L0_ffn_br, "W1": L0_ffn_W1, "b1": L0_ffn_b1,
        "W2": L0_ffn_W2, "b2": L0_ffn_b2,
    }
    weights = {k: np.ascontiguousarray(np.asarray(v, dtype=np.float32))
               for k, v in weights.items()}

    nc = _get_program(NCH_FULL)
    in_maps = []
    for b in range(B):
        m = {"x_in": np.ascontiguousarray(x[b])}
        m.update(weights)
        in_maps.append(m)
    res = run_bass_kernel_spmd(nc, in_maps, core_ids=list(range(B)))
    out = np.stack([res.results[b]["y_out"] for b in range(B)], axis=0)
    return out
